# version 01
# baseline (speedup 1.0000x reference)
import numpy as np
import ml_dtypes
import concourse.bass as bass
import concourse.mybir as mybir
from concourse.bass_utils import run_bass_kernel_spmd

F32 = mybir.dt.float32
BF16 = mybir.dt.bfloat16
AF = mybir.ActivationFunctionType
ALU = mybir.AluOpType
BF = ml_dtypes.bfloat16

P = 128
NT = 2048
NCH = NT // P
G = 8
D = 1024
F4 = 4096
TS = 512
NSUB = NT // TS
MAGIC = np.float32(1.5 * 2 ** 23)
PI = float(np.pi)

DEBUG = False
TAPS = []


class Emit:
    ENGS = ("tensor", "vector", "scalar", "gpsimd", "sync")

    def __init__(self):
        self.q = {e: [] for e in self.ENGS}
        self.cnt = {}
        self.lastw = {e: {} for e in self.ENGS}

    def n(self, sem):
        return self.cnt.get(sem, 0)

    def op(self, eng, fn, inc=None, waits=(), step=1):
        wl = []
        for (s, v) in waits:
            if v is None or v <= 0:
                continue
            if self.lastw[eng].get(s, -1) >= v:
                continue
            self.lastw[eng][s] = v
            wl.append((s, v))
        self.q[eng].append((wl, fn, inc, step))
        if inc is not None:
            self.cnt[inc] = self.cnt.get(inc, 0) + step
            return (inc, self.cnt[inc])
        return None

    def replay(self, eng_name, eng, sems):
        for (wl, fn, inc, step) in self.q[eng_name]:
            for (s, v) in wl:
                eng.wait_ge(sems[s], v)
            inst = fn(eng)
            if inc is not None and inst is not None:
                inst.then_inc(sems[inc], step)


class SAlloc:

    def __init__(self, nc, total):
        self.nc = nc
        self.base = (nc._sbuf_addr_for_side("left") + 63) // 64 * 64
        self.free = [(0, total)]
        self.live = {}
        self.peak = 0
        self.used = 0

    def alloc(self, name, shape, dtype, align=64):
        nbytes = int(np.prod(shape[1:])) * dtype.itemsize if hasattr(dtype, "itemsize") else None
        esz = {F32: 4, BF16: 2}[dtype]
        nbytes = int(np.prod(shape[1:])) * esz
        for i, (off, sz) in enumerate(self.free):
            aoff = (off + align - 1) // align * align
            pad = aoff - off
            if sz >= nbytes + pad:
                if pad:
                    self.free[i] = (off, pad)
                    self.free.insert(i + 1, (aoff + nbytes, sz - pad - nbytes))
                else:
                    self.free[i] = (aoff + nbytes, sz - nbytes)
                if self.free[i + (1 if pad else 0)][1] == 0:
                    self.free.pop(i + (1 if pad else 0))
                t = self.nc.alloc_sbuf_tensor_at(name, list(shape), dtype, offset=self.base + aoff)
                self.live[name] = (aoff, nbytes, t)
                self.used += nbytes
                self.peak = max(self.peak, self.high())
                return t
        raise MemoryError(f"SBUF alloc failed for {name} ({nbytes}B); free={self.free}")

    def high(self):
        hi = 0
        for (off, nb, _) in self.live.values():
            hi = max(hi, off + nb)
        return hi

    def freet(self, *names):
        for name in names:
            off, nb, _t = self.live.pop(name)
            self.free.append((off, nb))
        self.free.sort()
        out = []
        for seg in self.free:
            if out and out[-1][0] + out[-1][1] == seg[0]:
                out[-1] = (out[-1][0], out[-1][1] + seg[1])
            else:
                out.append(list(seg))
        self.free = [tuple(s) for s in out]


def build_nc(taps=()):
    nc = bass.Bass("TRN2", target_bir_lowering=False)
    em = Emit()
    OP = em.op

    dx = nc.dram_tensor("x", [NT, D], F32, kind="ExternalInput").ap()
    wnames2 = dict(Wq=[D, D], Wk=[D, D], Wv=[D, D], Wg=[D, D], Wa=[D, 2 * D], Wo=[D, D],
                   W1=[D, F4], W2=[F4, D],
                   fWr=[D, D], fWv=[D, D], fWo=[D, D], fWs=[D, D],
                   fW1=[D, D], fW2=[D, D], fW3=[D, D], W7h=[D, 8])
    dw = {k: nc.dram_tensor(k, v, F32, kind="ExternalInput").ap() for k, v in wnames2.items()}
    dK = nc.dram_tensor("fK", [2, D, F4], F32, kind="ExternalInput").ap()
    dV = nc.dram_tensor("fV", [2, F4, D], F32, kind="ExternalInput").ap()
    vnames = dict(b1=[F4], b2=[D], ln1s=[D], ln1b=[D], ln2s=[D], ln2b=[D],
                  fln1w=[D], fln1b=[D], fln2w=[D], fln2b=[D], caps=[3])
    dv = {k: nc.dram_tensor(k, v, F32, kind="ExternalInput").ap() for k, v in vnames.items()}
    dsel = nc.dram_tensor("selv", [P, 8], F32, kind="ExternalInput").ap()
    dide = nc.dram_tensor("iden", [P, P], F32, kind="ExternalInput").ap()
    dout = nc.dram_tensor("out", [NT, D], F32, kind="ExternalOutput").ap()
    d_ygab = nc.dram_tensor("ygab", [3, G, P, NT], BF16).ap()
    d_smy = nc.dram_tensor("smy", [P, 32], F32)
    d_gath = nc.dram_tensor("gath", [8 * P, 32], F32)
    d_x1 = nc.dram_tensor("x1sp", [G, P, NT], BF16).ap()
    d_x2 = nc.dram_tensor("x2sp", [G, P, NT], BF16).ap()

    taps_out = {}
    if taps:
        pass

    tap_tensors = {}

    def tapdecl(name, shape, dtype):
        t = nc.dram_tensor("tap_" + name, list(shape), dtype, kind="ExternalOutput").ap()
        tap_tensors[name] = dtype
        return t

    sa = SAlloc(nc, nc.sbuf_bytes_remaining - 1024)

    iden = sa.alloc("iden", [P, P], F32)
    onec = sa.alloc("onec", [1, P], F32)
    onecb = sa.alloc("onecb", [1, P], BF16)
    onespp = sa.alloc("onespp", [P, 1], BF16)
    selv = sa.alloc("selv", [P, 8], F32)
    smy = sa.alloc("smy", [P, 32], F32)
    gath = sa.alloc("gath", [P, 8 * 32], F32)
    Hrp = sa.alloc("Hrp", [P, 8], F32)
    Hip = sa.alloc("Hip", [P, 8], F32)
    Hin = sa.alloc("Hin", [P, 8], F32)
    ppv = {}
    for k in ("b2", "ln1b", "ln2s", "ln2b", "ln1s", "fln1w", "fln1b", "fln2w", "fln2b"):
        ppv[k] = sa.alloc("pp_" + k, [P, 8], F32)
    ppv["b1"] = sa.alloc("pp_b1", [P, 32], F32)
    csum2 = sa.alloc("csum2", [P, 8], F32)
    ppb = {}
    for k in ("ln1b", "fln1b", "fln2b"):
        ppb[k] = sa.alloc("ppb_" + k, [P, 8], BF16)
    capsr = sa.alloc("capsr", [1, 4], F32)
    pihalf = sa.alloc("pihalf", [P, 1], F32)
    capsB = sa.alloc("capsB", [P, 4], F32)
    cv_gelu = sa.alloc("cv_gelu", [P, 32], F32)
    cv_Wr = sa.alloc("cv_Wr", [P, 8], F32)
    cv_Wv = sa.alloc("cv_Wv", [P, 8], F32)
    cv_Ws = sa.alloc("cv_Ws", [P, 8], F32)
    cv_W1f = sa.alloc("cv_W1f", [P, 8], F32)
    cv_K = [sa.alloc(f"cv_K{e}", [P, 32], F32) for e in range(2)]
    cv_7 = sa.alloc("cv_7", [P, 1], F32)
    rowbuf = sa.alloc("rowbuf", [1, 2 * NT], F32)
    sqbuf = sa.alloc("sqbuf", [P, G * TS], BF16)
    rowbf = sa.alloc("rowbf", [1, 2 * NT], BF16)
    S2 = {"TT": sa.alloc("q_tt", [P, TS], F32), "TT2": sa.alloc("q_tt2", [P, TS], F32)}
    TMP1 = sa.alloc("ccT1", [P, 8], F32)
    TMP2 = sa.alloc("ccT2", [P, 8], F32)
    TMP3 = sa.alloc("ccT3", [P, 8], F32)

    pb = [nc.alloc_psum_tensor(f"pb{i}", [P, TS], F32) for i in range(8)]

    class Banks:
        def __init__(self):
            self.freetok = [None] * 8
            self.rr = 0

        def take(self, n=1):
            out = []
            for _ in range(n):
                i = self.rr % 8
                self.rr += 1
                out.append((i, pb[i], self.freetok[i]))
            return out if n > 1 else out[0]

        def set_free(self, i, tok):
            self.freetok[i] = tok

    bk = Banks()

    def wload_args(dst2d, src2d, ncols):
        return dict(out=dst2d.rearrange("p (k o) -> p k o", o=ncols),
                    in_=src2d.rearrange("(k p) o -> p k o", p=P))

    def mmg(bank_i, bank_ap, pairs, waits=(), free_wait=None, out_slice=None):
        o = bank_ap[:, :] if out_slice is None else out_slice
        wl = list(waits)
        if free_wait is not None:
            wl.append(free_wait)
        tok = None
        nn = len(pairs)
        for idx, (l, r) in enumerate(pairs):
            tok = OP("tensor",
                     (lambda e, l=l, r=r, o=o, st=(idx == 0), sp=(idx == nn - 1):
                      e.matmul(o, l, r, start=st, stop=sp)),
                     inc="t" if idx == nn - 1 else None,
                     waits=wl if idx == 0 else ())
        return tok

    def vop(fn, waits=(), eng="vector", inc=None):
        inc = {"vector": "v", "scalar": "s", "gpsimd": "g"}[eng] if inc is None else inc
        return OP(eng, fn, inc=inc, waits=waits)

    def vpad(n=1):
        for _ in range(n):
            OP("vector", lambda e: e.nop(cycle_cnt=512))

    def vpad_tok():
        vpad()
        return vop(lambda e: e.nop(cycle_cnt=64))

    def dma(eng, fn, waits=()):
        sem = {"sync": "dy", "scalar": "ds", "gpsimd": "dg"}[eng]
        return OP(eng, fn, inc=sem, waits=waits, step=16)

    tap_tokens = []

    def tap(name, sb_ap, shape, waits, dtype=F32):
        if name not in taps:
            return None
        t = tapdecl(name, shape, dtype)
        tok = dma("sync", lambda e, t=t, s=sb_ap: e.dma_start(out=t, in_=s), waits=waits)
        tap_tokens.append(tok)
        return tok

    ld0 = []
    ld0.append(dma("scalar", lambda e: e.dma_start(out=iden[:, :], in_=dide[:, :])))
    ld0.append(dma("scalar", lambda e: e.dma_start(out=selv[:, :], in_=dsel[:, :])))
    vstage = sa.alloc("vstage", [32, P], F32)
    pp_done = {}
    mem1 = vop(lambda e: e.memset(onec[:, :], 1.0))
    prev_evac = None
    for k in ("b2", "ln1b", "ln2s", "ln2b", "ln1s", "fln1w", "fln1b", "fln2w", "fln2b", "b1"):
        rows = 32 if k == "b1" else 8
        w = [ld0[0], prev_evac] if prev_evac else [ld0[0]]
        lt = dma("scalar", lambda e, k=k, rows=rows: e.dma_start(
            out=vstage[0:rows, :], in_=dv[k].rearrange("(k p) -> k p", p=P)),
            waits=[prev_evac] if prev_evac else [])
        bi, bap, frw = bk.take()
        tt = OP("tensor", lambda e, rows=rows, bap=bap: e.transpose(
            bap[:, 0:rows], vstage[0:rows, :], iden[0:rows, 0:rows]), inc="t",
            waits=[lt, ld0[0], frw] if frw else [lt, ld0[0]])
        ev = vop(lambda e, k=k, rows=rows, bap=bap: e.tensor_copy(
            ppv[k][:, 0:rows], bap[:, 0:rows]), waits=[tt])
        bk.set_free(bi, ev)
        pp_done[k] = ev
        prev_evac = ev
    ld0.append(dma("scalar", lambda e: e.dma_start(out=capsr[0:1, 0:3], in_=dv["caps"][None, :])))
    lds_tok = ld0[-1]
    for k in ("ln1b", "fln1b", "fln2b"):
        pp_done["b_" + k] = vop(lambda e, k=k: e.tensor_copy(ppb[k][:, :], ppv[k][:, :]),
                                waits=[pp_done[k]])
    mem2 = vop(lambda e: e.memset(onecb[:, :], 1.0))
    mem3 = vop(lambda e: e.memset(onespp[:, :], 1.0))
    mem4 = vop(lambda e: e.memset(pihalf[:, :], PI / 2))
    t_csum2 = vop(lambda e: e.tensor_tensor(csum2[:, :], ppv["b2"][:, :], ppv["ln1b"][:, :], ALU.add),
                  waits=[pp_done["b2"], pp_done["ln1b"]])
    bi, bap, frw = bk.take()
    capg = mmg(bi, bap, [(onec[0:1, :], capsr[0:1, 0:3])],
               waits=[mem1, ld0[-1]], out_slice=bap[:, 0:3])
    t_capb = vop(lambda e, b=bap: e.tensor_copy(capsB[:, 0:3], b[:, 0:3]), waits=[capg])
    bk.set_free(bi, t_capb)

    xch = [sa.alloc(f"xch{i}", [P, D], F32) for i in range(2)]
    sqch = sa.alloc("sqch", [P, D], BF16)
    st_mu = sa.alloc("st_mu", [P, NCH], F32)
    st_q = sa.alloc("st_q", [P, NCH], F32)
    st_r = sa.alloc("st_r", [P, NCH], F32)

    xload = [None] * NCH
    stats_done = [None] * NCH
    for ch in range(NCH):
        b = ch % 2
        w = []
        if ch >= 2:
            w.append(stats_done[ch - 2])
        xload[ch] = dma("scalar", lambda e, ch=ch, b=b: e.dma_start(
            out=xch[b][:, :], in_=dx[ch * P:(ch + 1) * P, :]), waits=w)
        t1 = vop(lambda e, b=b, ch=ch: e.tensor_reduce(
            st_mu[:, ch:ch + 1], xch[b][:, :], mybir.AxisListType.X, ALU.add),
            waits=[xload[ch]])
        t2 = vop(lambda e, b=b: e.activation(sqch[:, :], xch[b][:, :], AF.Square),
                 eng="scalar", waits=[xload[ch]])
        t3 = vop(lambda e, ch=ch: e.tensor_reduce(
            st_q[:, ch:ch + 1], sqch[:, :], mybir.AxisListType.X, ALU.add), waits=[t2])
        stats_done[ch] = t3 if t3[1] > t1[1] else t1
    r1 = vop(lambda e: e.tensor_scalar(st_mu[:, :], st_mu[:, :], 1.0 / D, None, ALU.mult),
             waits=[stats_done[NCH - 1]])
    r2 = vop(lambda e: e.tensor_scalar(st_q[:, :], st_q[:, :], 1.0 / D, None, ALU.mult))
    r3 = vop(lambda e: e.tensor_tensor(st_r[:, :], st_mu[:, :], st_mu[:, :], ALU.mult))
    vpad()
    r4 = vop(lambda e: e.tensor_tensor(st_q[:, :], st_q[:, :], st_r[:, :], ALU.subtract))
    vpad()
    r5 = vop(lambda e: e.tensor_scalar(st_q[:, :], st_q[:, :], 1e-5, None, ALU.add))
    vpad()
    r6 = vop(lambda e: e.activation(st_q[:, :], st_q[:, :], AF.Sqrt), eng="scalar",
             waits=[r5])
    r7 = vop(lambda e: e.reciprocal(st_r[:, :], st_q[:, :]), waits=[r6])
    ln0_row = vpad_tok()

    xln0 = [sa.alloc(f"xln0_{g}", [P, NT], BF16) for g in range(G)]
    xlt = [sa.alloc(f"xlt{i}", [P, D], BF16) for i in range(2)]
    xln0_tp = []
    norm_done = [None] * NCH
    for ch in range(NCH):
        b = ch % 2
        w = [ln0_row]
        if ch >= 2:
            w.append(xln0_tp[ch - 2])
        xload2 = dma("scalar", lambda e, ch=ch, b=b: e.dma_start(
            out=xch[b][:, :], in_=dx[ch * P:(ch + 1) * P, :]), waits=w)
        nd = vop(lambda e, b=b, ch=ch: e.tensor_scalar(
            xlt[b][:, :], xch[b][:, :], st_mu[:, ch:ch + 1], st_r[:, ch:ch + 1],
            ALU.subtract, ALU.mult), waits=[xload2])
        norm_done[ch] = nd
        last = None
        for g in range(G):
            last = dma("sync", lambda e, g=g, ch=ch, b=b: e.dma_start(
                out=xln0[g][:, ch * P:(ch + 1) * P],
                in_=xlt[b][:, g * P:(g + 1) * P], transpose=True), waits=[nd])
        xln0_tp.append(last)
    xln0_done = xln0_tp[-1]
    tap("P_xln0", xln0[0][:, :], [P, NT], [xln0_done], BF16)

    sa.freet("xch0", "xch1", "sqch", "xlt0", "xlt1")
    wg_bufs = [sa.alloc(f"wgP{i}", [P, 6 * 8 * P], BF16) for i in range(2)]
    S = {k: sa.alloc("sl_" + k, [P, NT], F32) for k in
         ("KV", "AR", "AI", "M", "R", "TH", "CI", "SIN", "COS", "T1")}
    S["Q"] = sa.alloc("sl_Q", [P, NT], BF16)
    S["GG"] = sa.alloc("sl_GG", [P, NT], BF16)
    Gbf = sa.alloc("sl_Gbf", [P, NT], BF16)
    YB = sa.alloc("sl_YB", [P, NT], BF16)
    GAB = sa.alloc("sl_GAB", [P, NT], BF16)
    GBB = sa.alloc("sl_GBB", [P, NT], BF16)

    def wslice(buf, proj, k):
        off = (proj * 8 + k) * P
        return buf[:, off:off + P]

    wP_load = [None, None]
    smy_parts = []
    spill_toks = []
    prev_spill = {}
    og_scan_done = [None] * G

    for og in range(G):
        b = og % 2
        w = [xln0_done]
        if og >= 2:
            w.append(og_scan_done[og - 2])
        projsrc = [
            (dw["Wq"], og * P), (dw["Wk"], og * P), (dw["Wv"], og * P),
            (dw["Wg"], og * P), (dw["Wa"], og * P), (dw["Wa"], D + og * P)]
        ldt = None
        for pi, (src, coff) in enumerate(projsrc):
            ldt = dma("gpsimd", lambda e, src=src, coff=coff, b=b, pi=pi: e.dma_start(
                **wload_args(wg_bufs[b][:, pi * 8 * P:(pi + 1) * 8 * P],
                             src[:, coff:coff + P], P)), waits=w)
        wP_load[b] = ldt

        if og == 0:
            tw = tap("P_wq", wg_bufs[0][:, 0:8 * P], [P, 8 * P], [ldt], BF16)
            tw2 = tap("P_war", wg_bufs[0][:, 4 * 8 * P:5 * 8 * P], [P, 8 * P], [ldt], BF16)
        evac_toks = []
        for ts_i in range(NSUB):
            sl = slice(ts_i * TS, (ts_i + 1) * TS)
            toks = {}
            banks = {}
            xw = [xln0_tp[min((ts_i + 1) * NSUB - 1, NCH - 1)]]
            xw = [xln0_tp[ts_i * 4 + 3]]
            evw = [og_scan_done[og - 1]] if og >= 1 else []
            if og >= 1 and tap_tokens:
                evw = evw + [tap_tokens[-1]]
            for pi, pname in enumerate(("q", "k", "v", "g", "ar", "ai")):
                bi, bap, frw = bk.take()
                pairs = [(wslice(wg_bufs[b], pi, k), xln0[k][:, sl]) for k in range(8)]
                toks[pname] = mmg(bi, bap, pairs, waits=[ldt] + xw, free_wait=frw)
                banks[pname] = (bi, bap)
            e0 = vop(lambda e, c=banks["v"][1], o=S["KV"], sl=sl:
                     e.activation(o[:, sl], c[:, :], AF.Copy), eng="scalar",
                     waits=[toks["v"]] + evw)
            e1 = vop(lambda e, a=banks["k"][1], o=S["KV"], sl=sl:
                     e.tensor_tensor(o[:, sl], a[:, :], o[:, sl], ALU.mult),
                     waits=[toks["k"], e0])
            e2 = vop(lambda e, a=banks["q"][1], o=S["Q"], sl=sl:
                     e.activation(o[:, sl], a[:, :], AF.Copy), eng="scalar",
                     waits=[toks["q"]] + evw)
            e3 = vop(lambda e, a=banks["g"][1], o=S["GG"], sl=sl:
                     e.activation(o[:, sl], a[:, :], AF.Copy), eng="scalar",
                     waits=[toks["g"]] + evw)
            e4 = vop(lambda e, a=banks["ar"][1], o=S["AR"], sl=sl:
                     e.activation(o[:, sl], a[:, :], AF.Copy), eng="scalar",
                     waits=[toks["ar"]] + evw)
            e5 = vop(lambda e, a=banks["ai"][1], o=S["AI"], sl=sl:
                     e.activation(o[:, sl], a[:, :], AF.Copy), eng="scalar",
                     waits=[toks["ai"]] + evw)
            e6 = vop(lambda e, a=banks["ar"][1], o=S["M"], sl=sl:
                     e.activation(o[:, sl], a[:, :], AF.Square), eng="scalar",
                     waits=[toks["ar"]] + evw)
            e7 = vop(lambda e, a=banks["ai"][1], o=S["T1"], sl=sl:
                     e.activation(o[:, sl], a[:, :], AF.Square), eng="scalar",
                     waits=[toks["ai"]] + evw)
            e8 = vop(lambda e, o=S["M"], t=S["T1"], sl=sl:
                     e.tensor_tensor(o[:, sl], o[:, sl], t[:, sl], ALU.add),
                     waits=[e6, e7])
            evac_toks += [e1, e2, e3, e4, e5, e8]
            bk.set_free(banks["k"][0], e1)
            bk.set_free(banks["v"][0], e0)
            bk.set_free(banks["q"][0], e2)
            bk.set_free(banks["g"][0], e3)
            bk.set_free(banks["ar"][0], e6)
            bk.set_free(banks["ai"][0], e7)
        ev = evac_toks[-1]
        evv = [t for t in evac_toks]
        tkv = tap("P_kv", S["KV"][:, :], [P, NT], evv) if og == 0 else None
        tar = tap("P_ar", S["AR"][:, :], [P, NT], evv) if og == 0 else None
        tq = tap("P_q", S["Q"][:, :], [P, NT], evv, BF16) if og == 0 else None

        a1 = vop(lambda e: e.activation(S["M"][:, :], S["M"][:, :], AF.Sqrt),
                 eng="scalar", waits=evv)
        a2 = vop(lambda e: e.activation(S["R"][:, :], S["M"][:, :], AF.Sigmoid),
                 eng="scalar", waits=[a1])
        a3 = vop(lambda e: e.activation(S["T1"][:, :], S["GG"][:, :], AF.Sigmoid),
                 eng="scalar", waits=evv)
        g1 = vop(lambda e: e.tensor_tensor(S["GG"][:, :], S["GG"][:, :], S["T1"][:, :], ALU.mult),
                 waits=[a3])
        g2 = vop(lambda e: e.tensor_tensor(Gbf[:, :], S["Q"][:, :], S["GG"][:, :], ALU.mult),
                 waits=[g1])
        d1 = vop(lambda e: e.tensor_tensor(S["M"][:, :], S["M"][:, :], S["AR"][:, :], ALU.add),
                 waits=[a2] + ([tar] if og == 0 and tar else []))
        d2 = vop(lambda e: e.reciprocal(S["M"][:, :], S["M"][:, :]), waits=[d1])
        d3 = vop(lambda e: e.tensor_tensor(S["TH"][:, :], S["AI"][:, :], S["M"][:, :], ALU.mult),
                 waits=[d2])
        d4 = vop(lambda e: e.activation(S["TH"][:, :], S["TH"][:, :], AF.Arctan),
                 eng="scalar", waits=[d3])
        sc1 = vop(lambda e: e.tensor_tensor_scan(
            S["TH"][:, :], S["TH"][:, :], S["TH"][:, :], 0.0, ALU.add, ALU.bypass),
            waits=[d4])
        q1 = vop(lambda e: e.tensor_scalar(S["T1"][:, :], S["TH"][:, :], 1.0 / PI, float(MAGIC),
                                           ALU.mult, ALU.add), waits=[sc1])
        q2 = vop(lambda e: e.tensor_scalar(S["T1"][:, :], S["T1"][:, :], -float(MAGIC), None, ALU.add))
        q3 = vop(lambda e: e.scalar_tensor_tensor(
            S["TH"][:, :], S["T1"][:, :], -PI, S["TH"][:, :], ALU.mult, ALU.add), waits=[q2, sc1])
        s1 = vop(lambda e: e.activation(S["SIN"][:, :], S["TH"][:, :], AF.Sin, scale=2.0),
                 eng="scalar", waits=[q3])
        s2 = vop(lambda e: e.activation(S["T1"][:, :], S["TH"][:, :], AF.Abs),
                 eng="scalar", waits=[q3])
        s3 = vop(lambda e: e.activation(S["COS"][:, :], S["T1"][:, :], AF.Sin, scale=-2.0,
                                        bias=pihalf[:, 0:1]), eng="scalar", waits=[s2, mem4])
        c1 = vop(lambda e: e.tensor_tensor(S["CI"][:, :], S["KV"][:, :], S["SIN"][:, :], ALU.mult),
                 waits=[s1])
        c2 = vop(lambda e: e.tensor_tensor(S["KV"][:, :], S["KV"][:, :], S["COS"][:, :], ALU.mult),
                 waits=[s3, c1] + ([tkv] if og == 0 and tkv else []))
        sc2 = vop(lambda e: e.tensor_tensor_scan(
            S["KV"][:, :], S["R"][:, :], S["KV"][:, :], 0.0, ALU.mult, ALU.add), waits=[c2])
        sc3 = vop(lambda e: e.tensor_tensor_scan(
            S["CI"][:, :], S["R"][:, :], S["CI"][:, :], 0.0, ALU.mult, ALU.add), waits=[c1, sc2])
        sc4 = vop(lambda e: e.tensor_tensor_scan(
            S["R"][:, :], S["R"][:, :], S["R"][:, :], 1.0, ALU.mult, ALU.bypass), waits=[sc3])
        L = NT - 1
        sm = []
        h1 = vop(lambda e: e.tensor_tensor(S["T1"][:, :], S["COS"][:, :], S["KV"][:, :], ALU.mult),
                 waits=[sc2] + sm)
        h2 = vop(lambda e: e.tensor_tensor(S["M"][:, :], S["SIN"][:, :], S["CI"][:, :], ALU.mult),
                 waits=[sc3])
        h3 = vop(lambda e: e.tensor_tensor(S["T1"][:, :], S["T1"][:, :], S["M"][:, :], ALU.add))
        h4 = vop(lambda e: e.tensor_tensor(YB[:, :], Gbf[:, :], S["T1"][:, :], ALU.mult),
                 waits=[g2])
        h5 = vop(lambda e: e.tensor_tensor(S["M"][:, :], Gbf[:, :], S["R"][:, :], ALU.mult),
                 waits=[sc4])
        h6 = vop(lambda e: e.tensor_tensor(GAB[:, :], S["M"][:, :], S["COS"][:, :], ALU.mult))
        h7 = vop(lambda e: e.tensor_tensor(GBB[:, :], S["M"][:, :], S["SIN"][:, :], ALU.mult))
        sm.append(vop(lambda e, og=og: e.tensor_copy(
            smy[:, og * 4:og * 4 + 1], S["T1"][:, L:]), waits=[h3]))
        sm.append(vop(lambda e, og=og: e.tensor_tensor(
            smy[:, og * 4 + 1:og * 4 + 2], S["SIN"][:, L:], S["KV"][:, L:], ALU.mult),
            waits=[h7]))
        sm.append(vop(lambda e, og=og: e.tensor_tensor(
            S["T1"][:, 0:1], S["COS"][:, L:], S["CI"][:, L:], ALU.mult)))
        sm.append(vop(lambda e, og=og: e.tensor_tensor(
            smy[:, og * 4 + 2:og * 4 + 3], S["R"][:, L:], S["COS"][:, L:], ALU.mult)))
        sm.append(vop(lambda e, og=og: e.tensor_tensor(
            smy[:, og * 4 + 3:og * 4 + 4], S["R"][:, L:], S["SIN"][:, L:], ALU.mult)))
        sm.append(vop(lambda e, og=og: e.tensor_tensor(
            smy[:, og * 4 + 1:og * 4 + 2], smy[:, og * 4 + 1:og * 4 + 2], S["T1"][:, 0:1],
            ALU.subtract)))
        smy_parts.append(vpad_tok())
        sp1 = dma("scalar", lambda e, og=og: e.dma_start(
            out=d_ygab[0, og, :, :], in_=YB[:, :]), waits=[h4])
        sp2 = dma("scalar", lambda e, og=og: e.dma_start(
            out=d_ygab[1, og, :, :], in_=GAB[:, :]), waits=[h6])
        sp3 = dma("scalar", lambda e, og=og: e.dma_start(
            out=d_ygab[2, og, :, :], in_=GBB[:, :]), waits=[h7])
        spill_toks += [sp1, sp2, sp3]
        og_scan_done[og] = sm[-1]
        if og == 0:
            tap("P_cos", S["COS"][:, :], [P, NT], [h7])
            tap("P_sin", S["SIN"][:, :], [P, NT], [h7])
            tap("P_gi", S["CI"][:, :], [P, NT], [h7])
            tap("P_r", S["R"][:, :], [P, NT], [sc4])
            tap("P_phi_sin", S["SIN"][:, :], [P, NT], [s1])
            tap("P_gr", S["KV"][:, :], [P, NT], [sc2])
            tap("P_Y0", YB[:, :], [P, NT], [h4], BF16)
            tap("P_GA", GAB[:, :], [P, NT], [h6], BF16)


    smy_st = dma("scalar", lambda e: e.dma_start(out=d_smy.ap()[:, :], in_=smy[:, :]),
                 waits=smy_parts + [og_scan_done[G - 1]])
    cct = OP("gpsimd", lambda e: e.collective_compute(
        "AllGather", ALU.bypass, replica_groups=[list(range(8))],
        ins=[d_smy.ap().opt()], outs=[d_gath.ap().opt()]),
        inc="cc", waits=[smy_st])
    gld = dma("scalar", lambda e: e.dma_start(
        out=gath[:, :].rearrange("p (s f) -> p s f", f=32),
        in_=d_gath.ap().rearrange("(s p) f -> p s f", p=P)),
        waits=[("cc", 1)])
    tap("X_smy", smy[:, :], [P, 32], [smy_st])
    tap("X_gath", gath[:, :], [P, 256], [gld])
    z1 = vop(lambda e: e.memset(Hrp[:, :], 0.0))
    z2 = vop(lambda e: e.memset(Hip[:, :], 0.0))
    last = None
    for s_i in range(8):
        def gsl(comp, s_i=s_i):
            return gath[:, s_i * 32:(s_i + 1) * 32].rearrange(
                "p (o c) -> p o c", c=4)[:, :, comp:comp + 1]
        hr_s, hi_s, ar_s, ai_s = gsl(0), gsl(1), gsl(2), gsl(3)
        w0 = [gld, z1, z2] if s_i == 0 else [last]
        u1 = vop(lambda e, a=ar_s: e.tensor_tensor(TMP1[:, :], a, Hrp[:, :], ALU.mult), waits=w0)
        u2 = vop(lambda e, a=ai_s: e.tensor_tensor(TMP2[:, :], a, Hip[:, :], ALU.mult))
        v2 = vop(lambda e, a=ar_s: e.tensor_tensor(TMP3[:, :], a, Hip[:, :], ALU.mult))
        u3 = vop(lambda e: e.tensor_tensor(TMP1[:, :], TMP1[:, :], TMP2[:, :], ALU.subtract))
        v1 = vop(lambda e, a=ai_s: e.tensor_tensor(TMP2[:, :], a, Hrp[:, :], ALU.mult))
        u4 = vop(lambda e, h=hr_s: e.tensor_tensor(TMP1[:, :], TMP1[:, :], h, ALU.add))
        v3 = vop(lambda e: e.tensor_tensor(TMP2[:, :], TMP2[:, :], TMP3[:, :], ALU.add))
        vpad()
        v4 = vop(lambda e, h=hi_s: e.tensor_tensor(TMP2[:, :], TMP2[:, :], h, ALU.add))
        b1 = vop(lambda e: e.tensor_tensor(TMP1[:, :], TMP1[:, :], Hrp[:, :], ALU.subtract))
        vpad()
        b3 = vop(lambda e: e.tensor_tensor(TMP2[:, :], TMP2[:, :], Hip[:, :], ALU.subtract))
        b2 = vop(lambda e, s_i=s_i: e.scalar_tensor_tensor(
            Hrp[:, :], TMP1[:, :], selv[:, s_i:s_i + 1], Hrp[:, :], ALU.mult, ALU.add))
        vpad()
        b4 = vop(lambda e, s_i=s_i: e.scalar_tensor_tensor(
            Hip[:, :], TMP2[:, :], selv[:, s_i:s_i + 1], Hip[:, :], ALU.mult, ALU.add))
        vpad()
        last = vop(lambda e: e.nop(cycle_cnt=64))
    hneg = vop(lambda e: e.tensor_scalar(Hin[:, :], Hip[:, :], -1.0, None, ALU.mult),
               waits=[last])
    carry_done = vpad_tok()
    if True:
        tap("X_Hrp", Hrp[:, :], [P, 8], [carry_done])

    sa.freet(*["sl_" + k for k in S.keys()], "sl_Gbf", "sl_YB", "sl_GAB", "sl_GBB",
             "wgP0", "wgP1", "st_mu", "st_q", "st_r")

    wo_sb = sa.alloc("wo_sb", [P, 8 * D], BF16)
    wo_ld = dma("gpsimd", lambda e: e.dma_start(
        **wload_args(wo_sb[:, :], dw["Wo"], D)), waits=[spill_toks[-1], smy_st])
    ygl = [sa.alloc(f"ygl{i}", [P, 3 * G * TS], BF16) for i in range(2)]
    ymix = [sa.alloc(f"ymix{i}", [P, G * TS], BF16) for i in range(2)]
    y1pre = sa.alloc("y1pre", [P, G * NT], BF16)

    def ygsl(b, kind, og):
        off = (kind * G + og) * TS
        return ygl[b][:, off:off + TS]

    yg_ld = [None] * NSUB
    q1_done = [None] * NSUB
    ymix_done = [None] * NSUB

    def emit_ygld(ts_i, w):
        b = ts_i % 2
        yg_ld[ts_i] = dma("scalar", lambda e, ts_i=ts_i, b=b: e.dma_start(
            out=ygl[b][:, :].rearrange("p (k g t) -> p k g t", g=G, t=TS),
            in_=d_ygab[:, :, :, ts_i * TS:(ts_i + 1) * TS].rearrange("k g p t -> p k g t")),
            waits=w)
    emit_ygld(0, [spill_toks[-1]])
    emit_ygld(1, [spill_toks[-1]])
    for ts_i in range(NSUB):
        b = ts_i % 2
        mt = []
        for og in range(G):
            m1 = vop(lambda e, b=b, og=og, ts_i=ts_i: e.scalar_tensor_tensor(
                ymix[b][:, og * TS:(og + 1) * TS], ygsl(b, 1, og), Hrp[:, og:og + 1],
                ygsl(b, 0, og), ALU.mult, ALU.add),
                waits=[yg_ld[ts_i], carry_done])
            m2 = vop(lambda e, b=b, og=og: e.scalar_tensor_tensor(
                ymix[b][:, og * TS:(og + 1) * TS], ygsl(b, 2, og), Hin[:, og:og + 1],
                ymix[b][:, og * TS:(og + 1) * TS], ALU.mult, ALU.add))
            mt.append(m2)
        if ts_i == 0:
            tq = tap("Q_ymix", ymix[0][:, :], [P, G * TS], [mt[-1]], BF16)
            if tq: mt.append(tq)
        ymix_done[ts_i] = mt[-1]
        last_wo = None
        for oo in range(G):
            bi, bap, frw = bk.take()
            pairs = [(wo_sb[:, k * D + oo * P:k * D + oo * P + P],
                      ymix[b][:, k * TS:(k + 1) * TS]) for k in range(8)]
            tok = mmg(bi, bap, pairs, waits=[wo_ld] + mt, free_wait=frw)
            ev = vop(lambda e, bap=bap, oo=oo, ts_i=ts_i: e.tensor_tensor(
                y1pre[:, oo * NT + ts_i * TS: oo * NT + (ts_i + 1) * TS],
                bap[:, :], xln0[oo][:, ts_i * TS:(ts_i + 1) * TS], ALU.add),
                waits=[tok])
            bk.set_free(bi, ev)
            last_wo = tok
            q1_done[ts_i] = ev
        if ts_i + 2 < NSUB:
            emit_ygld(ts_i + 2, [mt[-1], last_wo])
    if True:
        tap("Q_y1pre", y1pre[:, :], [P, G * NT], [q1_done[NSUB - 1]], BF16)

    row_mu = rowbuf[0:1, 0:NT]
    row_q = rowbuf[0:1, NT:2 * NT]
    rbf_r = rowbf[0:1, 0:NT]
    rbf_m = rowbf[0:1, NT:2 * NT]

    ln_bcast_last = [None]

    def fm_ln_stats(xsl_fn, waits, eps, extra_sq_w=None, name=""):
        stok = []
        prev_p2 = None
        for ts_i in range(NSUB):
            bi1, bap1, frw1 = bk.take()
            bi2, bap2, frw2 = bk.take()
            sq_toks = []
            for og in range(G):
                sq_toks.append(vop(
                    lambda e, og=og, ts_i=ts_i, x=xsl_fn(og, ts_i):
                    e.activation(sqbuf[:, og * TS:(og + 1) * TS], x, AF.Square),
                    eng="scalar",
                    waits=list(waits(og, ts_i)) + ([prev_p2] if prev_p2 else [])))
            p1 = mmg(bi1, bap1, [(onespp[:, 0:1], xsl_fn(k, ts_i)) for k in range(8)],
                     waits=[mem3] + [w for og in range(G) for w in waits(og, ts_i)],
                     free_wait=frw1, out_slice=bap1[0:1, :])
            p2 = mmg(bi2, bap2, [(onespp[:, 0:1], sqbuf[:, k * TS:(k + 1) * TS]) for k in range(8)],
                     waits=sq_toks, free_wait=frw2, out_slice=bap2[0:1, :])
            prev_p2 = p2
            c1 = vop(lambda e, bap=bap1, ts_i=ts_i: e.tensor_copy(
                row_mu[0:1, ts_i * TS:(ts_i + 1) * TS], bap[0:1, :]), waits=[p1])
            c2 = vop(lambda e, bap=bap2, ts_i=ts_i: e.tensor_copy(
                row_q[0:1, ts_i * TS:(ts_i + 1) * TS], bap[0:1, :]), waits=[p2])
            bk.set_free(bi1, c1)
            bk.set_free(bi2, c2)
            stok += [c1, c2]
        m1 = vop(lambda e: e.tensor_scalar(row_mu, row_mu, 1.0 / D, None, ALU.mult), waits=stok)
        m2 = vop(lambda e: e.tensor_scalar(row_q, row_q, 1.0 / D, None, ALU.mult))
        m3 = vop(lambda e: e.tensor_tensor(rbf_m, row_mu, row_mu, ALU.mult))
        m4 = vop(lambda e: e.tensor_tensor(row_q, row_q, rbf_m, ALU.subtract))
        m5 = vop(lambda e, eps=eps: e.tensor_scalar(row_q, row_q, eps, None, ALU.add))
        m6 = vop(lambda e: e.activation(row_q, row_q, AF.Sqrt), eng="scalar", waits=[m5])
        m7 = vop(lambda e: e.reciprocal(row_q, row_q), waits=[m6])
        m8 = vop(lambda e: e.tensor_tensor(row_mu, row_mu, row_q, ALU.mult))
        m9 = vop(lambda e: e.tensor_scalar(row_mu, row_mu, -1.0, None, ALU.mult))
        mc1 = vop(lambda e: e.tensor_copy(rbf_r, row_q),
                  waits=[m7, m8] + ([ln_bcast_last[0]] if ln_bcast_last[0] else []))
        mc2 = vop(lambda e: e.tensor_copy(rbf_m, row_mu), waits=[m9])
        return [mc1, mc2]

    def fm_ln_norm(xsl_fn, out_fn, row_toks, wpp=None, extra=None):
        outs = {}
        for ts_i in range(NSUB):
            bi1, bap1, frw1 = bk.take()
            bi2, bap2, frw2 = bk.take()
            t1 = mmg(bi1, bap1, [(onecb[0:1, :], rbf_r[0:1, ts_i * TS:(ts_i + 1) * TS])],
                     waits=list(row_toks) + [mem2], free_wait=frw1)
            t2 = mmg(bi2, bap2, [(onecb[0:1, :], rbf_m[0:1, ts_i * TS:(ts_i + 1) * TS])],
                     waits=list(row_toks) + [mem2], free_wait=frw2)
            ln_bcast_last[0] = t2
            lastv = None
            for og in range(G):
                x = xsl_fn(og, ts_i)
                o = out_fn(og, ts_i)
                if wpp is not None:
                    v1 = vop(lambda e, x=x, bap=bap1, og=og: e.scalar_tensor_tensor(
                        S2["TT"][:, 0:TS], x, wpp[:, og:og + 1], bap[:, :], ALU.mult, ALU.mult),
                        waits=[t1, t2])
                    v2 = vop(lambda e, o=o, bap=bap2, og=og: e.scalar_tensor_tensor(
                        o, bap[:, :], wpp[:, og:og + 1], S2["TT"][:, 0:TS], ALU.mult, ALU.add))
                else:
                    v1 = vop(lambda e, x=x, bap=bap1: e.scalar_tensor_tensor(
                        S2["TT"][:, 0:TS], x, 1.0, bap[:, :], ALU.bypass, ALU.mult),
                        waits=[t1, t2])
                    v2 = vop(lambda e, o=o, bap=bap2: e.scalar_tensor_tensor(
                        o, bap[:, :], 1.0, S2["TT"][:, 0:TS], ALU.bypass, ALU.add))
                lastv = v2
                outs[(og, ts_i)] = v2
            bk.set_free(bi1, lastv)
            bk.set_free(bi2, lastv)
        return outs

    sa.freet(*[f"xln0_{g}" for g in range(G)], "ygl0", "ygl1", "ymix0", "ymix1", "wo_sb")
    y1w = sa.alloc("y1w", [P, G * NT], BF16)
    ln1_rows = fm_ln_stats(
        lambda og, ts_i: y1pre[:, og * NT + ts_i * TS: og * NT + (ts_i + 1) * TS],
        lambda og, ts_i: [q1_done[ts_i]], 1e-6)
    tap("Q_ln1rows", rowbf[0:1, :], [1, 2 * NT], ln1_rows, BF16)
    ln1_norm = fm_ln_norm(
        lambda og, ts_i: y1pre[:, og * NT + ts_i * TS: og * NT + (ts_i + 1) * TS],
        lambda og, ts_i: y1w[:, og * NT + ts_i * TS: og * NT + (ts_i + 1) * TS],
        ln1_rows, wpp=ppv["ln1s"])
    if True:
        tap("Q_y1w", y1w[:, :], [P, G * NT], [ln1_norm[(G - 1, NSUB - 1)]], BF16)

    ln1_done = ln1_norm[(G - 1, NSUB - 1)]

    def slc(buf, og, ts_i):
        return buf[:, og * NT + ts_i * TS: og * NT + (ts_i + 1) * TS]

    def cvec_mms(wbuf, wcols, vec_pp, out_pp, ocol0, ncols, wwaits, kslice=None):
        bi, bap, frw = bk.take()
        toks = []
        for j in range(ncols):
            pairs = [(wbuf[:, k * wcols + j * P:k * wcols + j * P + P],
                      vec_pp[:, k:k + 1]) for k in range(8)]
            o = bap[:, j:j + 1]
            nn = len(pairs)
            for idx, (l, r) in enumerate(pairs):
                wl = (list(wwaits) + ([frw] if frw else [])) if (j == 0 and idx == 0) else ()
                t = OP("tensor", (lambda e, l=l, r=r, o=o, st_=(idx == 0), sp=(idx == nn - 1):
                                  e.matmul(o, l, r, start=st_, stop=sp)),
                       inc="t" if idx == nn - 1 else None, waits=wl)
            toks.append(t)
        ev = vop(lambda e, bap=bap, ocol0=ocol0, ncols=ncols: e.tensor_copy(
            out_pp[:, ocol0:ocol0 + ncols], bap[:, 0:ncols]), waits=[toks[-1]])
        bk.set_free(bi, ev)
        vpad()
        return toks[-1], vop(lambda e: e.nop(cycle_cnt=64))

    sa.freet("y1pre")
    FH = F4 // 2
    w1buf = sa.alloc("w1buf", [P, 8 * FH], BF16)
    w2buf = sa.alloc("w2buf", [P, 16 * D], BF16)
    h1b = sa.alloc("h1b", [P, 16 * TS], BF16)
    sum2 = sa.alloc("sum2", [P, G * NT], BF16)
    q2_done = None
    w1mm_last = {}
    w2mm_last = {}
    prev_half_tok = None
    for half in range(2):
        wl = [ln1_done] if half == 0 else [prev_half_tok]
        w1ld = dma("gpsimd", lambda e, half=half: e.dma_start(
            **wload_args(w1buf[:, :], dw["W1"][:, half * FH:(half + 1) * FH], FH)), waits=wl)
        w2ld = dma("gpsimd", lambda e, half=half: e.dma_start(
            **wload_args(w2buf[:, :], dw["W2"][half * FH:(half + 1) * FH, :], D)), waits=wl)
        cvt, cve = cvec_mms(w1buf, FH, ppb["ln1b"], cv_gelu, half * 16, 16, [w1ld, pp_done["b_ln1b"]])
        cb = vop(lambda e, half=half: e.tensor_tensor(
            cv_gelu[:, half * 16:(half + 1) * 16], cv_gelu[:, half * 16:(half + 1) * 16],
            ppv["b1"][:, half * 16:(half + 1) * 16], ALU.add), waits=[cve, pp_done["b1"]])
        vpad()
        cvg = vop(lambda e: e.nop(cycle_cnt=64))
        gelu_toks = {}
        for ts_i in range(NSUB):
            prev_w2 = w2mm_last.get(ts_i - 1)
            for og_h in range(16):
                bi, bap, frw = bk.take()
                pairs = [(w1buf[:, k * FH + og_h * P:k * FH + og_h * P + P],
                          slc(y1w, k, ts_i)) for k in range(8)]
                w_ = [w1ld, ln1_done]
                if frw: w_.append(frw)
                tok = mmg(bi, bap, pairs, waits=w_)
                gw = [tok, cvg]
                if prev_w2 is not None:
                    gw.append(prev_w2)
                gt = vop(lambda e, bap=bap, og_h=og_h, half=half: e.activation(
                    h1b[:, og_h * TS:(og_h + 1) * TS], bap[:, :], AF.Gelu_apprx_tanh,
                    bias=cv_gelu[:, half * 16 + og_h:half * 16 + og_h + 1]),
                    eng="scalar", waits=gw)
                gelu_toks[(og_h, ts_i)] = gt
                bk.set_free(bi, gt)
            for oo in range(G):
                bi, bap, frw = bk.take()
                pairs = [(w2buf[:, kh * D + oo * P:kh * D + oo * P + P],
                          h1b[:, kh * TS:(kh + 1) * TS]) for kh in range(16)]
                w_ = [w2ld] + [gelu_toks[(kh, ts_i)] for kh in range(16)]
                if frw: w_.append(frw)
                tok = mmg(bi, bap, pairs, waits=w_)
                w2mm_last[ts_i] = tok
                if half == 0:
                    ev = vop(lambda e, bap=bap, oo=oo, ts_i=ts_i: e.activation(
                        slc(sum2, oo, ts_i), bap[:, :], AF.Copy), eng="scalar",
                        waits=[tok])
                else:
                    ev1 = vop(lambda e, bap=bap, oo=oo, ts_i=ts_i: e.scalar_tensor_tensor(
                        S2["TT"][:, 0:TS], slc(sum2, oo, ts_i), csum2[:, oo:oo + 1],
                        bap[:, :], ALU.add, ALU.add), waits=[tok, t_csum2])
                    ev = vop(lambda e, oo=oo, ts_i=ts_i: e.tensor_tensor(
                        slc(sum2, oo, ts_i), S2["TT"][:, 0:TS], slc(y1w, oo, ts_i), ALU.add))
                bk.set_free(bi, ev)
                q2_done = ev
        prev_half_tok = q2_done if half == 1 else w2mm_last[NSUB - 1]
    tap("Q_sum2", sum2[:, :], [P, G * NT], [q2_done], BF16)

    sa.freet("w1buf", "w2buf", "h1b", "y1w")
    z = sa.alloc("z", [P, G * NT], BF16)
    x_fm = sa.alloc("x_fm", [P, G * NT], BF16)
    xs2 = [sa.alloc(f"xs2_{i}", [P, D], F32) for i in range(2)]
    xb2 = [sa.alloc(f"xb2_{i}", [P, D], BF16) for i in range(2)]
    xfm_tp = []
    for ch in range(NCH):
        b = ch % 2
        w = [q2_done, prev_half_tok]
        if ch >= 2:
            w.append(xfm_tp[ch - 2])
        xl = dma("scalar", lambda e, ch=ch, b=b: e.dma_start(
            out=xs2[b][:, :], in_=dx[ch * P:(ch + 1) * P, :]), waits=w)
        cd = vop(lambda e, b=b: e.tensor_copy(xb2[b][:, :], xs2[b][:, :]), waits=[xl])
        last = None
        for g in range(G):
            last = dma("sync", lambda e, g=g, ch=ch, b=b: e.dma_start(
                out=x_fm[:, g * NT + ch * P:g * NT + (ch + 1) * P],
                in_=xb2[b][:, g * P:(g + 1) * P], transpose=True), waits=[cd])
        xfm_tp.append(last)
    xfm_done = xfm_tp[-1]
    ln2_rows = fm_ln_stats(lambda og, ts_i: slc(sum2, og, ts_i),
                           lambda og, ts_i: [q2_done], 1e-6)
    ln2_norm = fm_ln_norm(lambda og, ts_i: slc(sum2, og, ts_i),
                          lambda og, ts_i: slc(z, og, ts_i), ln2_rows, wpp=ppv["ln2s"])
    zb_done = {}
    for ts_i in range(NSUB):
        for og in range(G):
            zb_done[(og, ts_i)] = vop(lambda e, og=og, ts_i=ts_i: e.tensor_scalar(
                slc(z, og, ts_i), slc(z, og, ts_i), ppv["ln2b"][:, og:og + 1], None,
                ALU.add), waits=[ln2_norm[(og, ts_i)], pp_done["ln2b"]])
    zdone = zb_done[(G - 1, NSUB - 1)]
    x1b = sa.alloc("x1b", [P, G * NT], BF16)
    o1_rows = fm_ln_stats(lambda og, ts_i: slc(z, og, ts_i),
                          lambda og, ts_i: [zb_done[(og, ts_i)]], 1e-5)
    o1_norm = {}
    x1_done = {}
    for ts_i in range(NSUB):
        bi1, bap1, frw1 = bk.take()
        bi2, bap2, frw2 = bk.take()
        t1 = mmg(bi1, bap1, [(onecb[0:1, :], rbf_r[0:1, ts_i * TS:(ts_i + 1) * TS])],
                 waits=list(o1_rows) + [mem2], free_wait=frw1)
        t2 = mmg(bi2, bap2, [(onecb[0:1, :], rbf_m[0:1, ts_i * TS:(ts_i + 1) * TS])],
                 waits=list(o1_rows) + [mem2], free_wait=frw2)
        ln_bcast_last[0] = t2
        lastv = None
        for og in range(G):
            v1 = vop(lambda e, og=og, ts_i=ts_i, bap=bap1: e.scalar_tensor_tensor(
                S2["TT"][:, 0:TS], slc(z, og, ts_i), 1.0, bap[:, :], ALU.bypass, ALU.mult),
                waits=[t1, t2])
            v2 = vop(lambda e, og=og, ts_i=ts_i, bap=bap2: e.scalar_tensor_tensor(
                S2["TT2"][:, 0:TS], bap[:, :], 1.0, S2["TT"][:, 0:TS], ALU.bypass, ALU.add))
            v3 = vop(lambda e, og=og, ts_i=ts_i: e.tensor_tensor(
                slc(x1b, og, ts_i), S2["TT2"][:, 0:TS], slc(x_fm, og, ts_i), ALU.add),
                waits=[xfm_done])
            lastv = v3
            x1_done[(og, ts_i)] = v3
        bk.set_free(bi1, lastv)
        bk.set_free(bi2, lastv)
    x1_all = x1_done[(G - 1, NSUB - 1)]
    x1_sp = dma("scalar", lambda e: e.dma_start(
        out=d_x1.rearrange("g p t -> p g t"),
        in_=x1b[:, :].rearrange("p (g t) -> p g t", t=NT)), waits=[x1_all])
    tap("Q_x1", x1b[:, :], [P, G * NT], [x1_all], BF16)

    sa.freet("sum2", "z", "x_fm", "xs2_0", "xs2_1", "xb2_0", "xb2_1")
    xln1 = sa.alloc("xln1", [P, G * NT], BF16)
    uw = sa.alloc("uw", [P, G * NT], BF16)
    o2_rows = fm_ln_stats(lambda og, ts_i: slc(x1b, og, ts_i),
                          lambda og, ts_i: [x1_done[(og, ts_i)]], 1e-5)
    o2_norm = fm_ln_norm(lambda og, ts_i: slc(x1b, og, ts_i),
                         lambda og, ts_i: slc(xln1, og, ts_i), o2_rows, wpp=None)
    f1_rows = fm_ln_stats(lambda og, ts_i: slc(xln1, og, ts_i),
                          lambda og, ts_i: [o2_norm[(og, ts_i)]], 1e-5)
    f1_norm = fm_ln_norm(lambda og, ts_i: slc(xln1, og, ts_i),
                         lambda og, ts_i: slc(uw, og, ts_i), f1_rows, wpp=ppv["fln1w"])
    uw_done = f1_norm[(G - 1, NSUB - 1)]
    tap("Q_uw", uw[:, :], [P, G * NT], [uw_done], BF16)

    sa.freet("x1b")
    wA = sa.alloc("wA", [P, 8 * D], BF16)
    wB = sa.alloc("wB", [P, 8 * D], BF16)
    tgate = sa.alloc("tgate", [P, G * NT], BF16)
    state = sa.alloc("state", [P, G * NT], BF16)
    ld_r = dma("gpsimd", lambda e: e.dma_start(**wload_args(wA[:, :], dw["fWr"], D)),
               waits=[uw_done, x1_sp])
    ld_v = dma("gpsimd", lambda e: e.dma_start(**wload_args(wB[:, :], dw["fWv"], D)))
    _, cv_r_tok = cvec_mms(wA, D, ppb["fln1b"], cv_Wr, 0, 8, [ld_r, pp_done["b_fln1b"]])
    _, cv_v_tok = cvec_mms(wB, D, ppb["fln1b"], cv_Wv, 0, 8, [ld_v, pp_done["b_fln1b"]])
    tg_toks = {}
    mm_rv_last = None
    prev_tg = None
    for ts_i in range(NSUB):
        for oo in range(G):
            bi1, bap1, frw1 = bk.take()
            bi2, bap2, frw2 = bk.take()
            pr_ = [(wA[:, k * D + oo * P:k * D + oo * P + P], slc(uw, k, ts_i)) for k in range(8)]
            pv_ = [(wB[:, k * D + oo * P:k * D + oo * P + P], slc(uw, k, ts_i)) for k in range(8)]
            w_ = [ld_r, uw_done] + ([frw1] if frw1 else [])
            tok1 = mmg(bi1, bap1, pr_, waits=w_)
            w_ = [ld_v] + ([frw2] if frw2 else [])
            tok2 = mmg(bi2, bap2, pv_, waits=w_)
            mm_rv_last = tok2
            rr_ = vop(lambda e, bap=bap1, oo=oo: e.activation(
                S2["TT"][:, 0:TS], bap[:, :], AF.Sigmoid, bias=cv_Wr[:, oo:oo + 1]),
                eng="scalar", waits=[tok1, cv_r_tok] + ([prev_tg] if prev_tg else []))
            tg = vop(lambda e, bap=bap2, oo=oo, ts_i=ts_i: e.scalar_tensor_tensor(
                slc(tgate, oo, ts_i), bap[:, :], cv_Wv[:, oo:oo + 1], S2["TT"][:, 0:TS],
                ALU.add, ALU.mult), waits=[tok2, cv_v_tok, rr_])
            tg_toks[(oo, ts_i)] = tg
            prev_tg = tg
            bk.set_free(bi1, rr_)
            bk.set_free(bi2, tg)
    ld_s = dma("gpsimd", lambda e: e.dma_start(**wload_args(wA[:, :], dw["fWs"], D)),
               waits=[mm_rv_last])
    _, cv_s_tok = cvec_mms(wA, D, ppb["fln1b"], cv_Ws, 0, 8, [ld_s, pp_done["b_fln1b"]])
    st_toks = {}
    mm_s = None
    for ts_i in range(NSUB):
        for oo in range(G):
            bi, bap, frw = bk.take()
            pairs = [(wA[:, k * D + oo * P:k * D + oo * P + P], slc(uw, k, ts_i)) for k in range(8)]
            w_ = [ld_s, uw_done] + ([frw] if frw else [])
            tok = mmg(bi, bap, pairs, waits=w_)
            mm_s = tok
            ev = vop(lambda e, bap=bap, oo=oo, ts_i=ts_i: e.activation(
                slc(state, oo, ts_i), bap[:, :], AF.Identity, bias=cv_Ws[:, oo:oo + 1]),
                eng="scalar", waits=[tok, cv_s_tok])
            st_toks[(oo, ts_i)] = ev
            bk.set_free(bi, ev)
    sa.freet("uw")
    x2b = sa.alloc("x2b", [P, G * NT], BF16)
    x2_done = {}
    ld_o2 = dma("gpsimd", lambda e: e.dma_start(**wload_args(wB[:, :], dw["fWo"], D)),
                waits=[mm_rv_last])
    mm_o2 = None
    for ts_i in range(NSUB):
        for oo in range(G):
            bi, bap, frw = bk.take()
            pairs = [(wB[:, k * D + oo * P:k * D + oo * P + P],
                      slc(tgate, k, ts_i)) for k in range(8)]
            w_ = [ld_o2] + [tg_toks[(k, ts_i)] for k in range(8)] + ([frw] if frw else [])
            tok = mmg(bi, bap, pairs, waits=w_)
            mm_o2 = tok
            t = vop(lambda e, bap=bap, oo=oo, ts_i=ts_i: e.tensor_tensor(
                slc(x2b, oo, ts_i), bap[:, :], slc(xln1, oo, ts_i), ALU.add),
                waits=[tok, o2_norm[(oo, ts_i)]])
            x2_done[(oo, ts_i)] = t
            bk.set_free(bi, t)
    x2_all = x2_done[(G - 1, NSUB - 1)]
    tap("Q_x2", x2b[:, :], [P, G * NT], [x2_all], BF16)
    x2_sp = dma("scalar", lambda e: e.dma_start(
        out=d_x2.rearrange("g p t -> p g t"),
        in_=x2b[:, :].rearrange("p (g t) -> p g t", t=NT)), waits=[x2_all])

    sa.freet("xln1", "tgate")
    hw = sa.alloc("hw", [P, G * NT], BF16)
    f2_rows = fm_ln_stats(lambda og, ts_i: slc(x2b, og, ts_i),
                          lambda og, ts_i: [x2_done[(og, ts_i)]], 1e-5)
    f2_norm = fm_ln_norm(lambda og, ts_i: slc(x2b, og, ts_i),
                         lambda og, ts_i: slc(hw, og, ts_i), f2_rows, wpp=ppv["fln2w"])
    hw_done = f2_norm[(G - 1, NSUB - 1)]
    tap("Q_hw", hw[:, :], [P, G * NT], [hw_done], BF16)

    w7sb = sa.alloc("w7sb", [P, 8 * 8], BF16)
    w7ld = dma("gpsimd", lambda e: e.dma_start(**wload_args(w7sb[:, :], dw["W7h"], 8)),
               waits=[mm_s])
    bi7, bap7, frw7 = bk.take()
    c7toks = []
    for k in range(8):
        c7toks.append(OP("tensor", (lambda e, k=k, bap=bap7:
                         e.matmul(bap[0:8, 0:1], w7sb[:, k * 8:(k + 1) * 8],
                                  ppb["fln2b"][:, k:k + 1], start=(k == 0), stop=(k == 7))),
                         inc="t" if k == 7 else None,
                         waits=[w7ld, pp_done["b_fln2b"]] + ([frw7] if frw7 and k == 0 else [])))
    c7e = vop(lambda e, bap=bap7: e.tensor_copy(cv_7[0:8, 0:1], bap[0:8, 0:1]),
              waits=[c7toks[-1]])
    bk.set_free(bi7, c7e)
    vpad()
    c7done = vop(lambda e: e.nop(cycle_cnt=64))
    s7sb = sa.alloc("s7sb", [8, TS], F32)
    s7tm = sa.alloc("s7tm", [P, 32], F32)
    sg4 = sa.alloc("sg4", [P, 16], F32)
    bidb = sa.alloc("bidb", [P, 12], F32)
    mskc = sa.alloc("mskc", [P, 12], F32)
    mrow = sa.alloc("mrow", [1, 3 * TS], F32)
    mskB = sa.alloc("mskB", [P, NSUB * 3 * TS], BF16)
    msk_done = [None] * NSUB
    for ts_i in range(NSUB):
        bi, bap, frw = bk.take()
        pairs = [(w7sb[:, k * 8:(k + 1) * 8], slc(hw, k, ts_i)) for k in range(8)]
        w_ = [w7ld] + [f2_norm[(k, ts_i)] for k in range(8)]
        if frw: w_.append(frw)
        tok = mmg(bi, bap, pairs, waits=w_, out_slice=bap[0:8, :])
        e1 = vop(lambda e, bap=bap: e.activation(
            s7sb[0:8, :], bap[0:8, :], AF.Identity, bias=cv_7[0:8, 0:1]),
            eng="scalar", waits=[tok, c7done])
        bk.set_free(bi, e1)
        biT, bapT, frwT = bk.take()
        ttoks = []
        for cj in range(4):
            ttoks.append(OP("tensor", (lambda e, cj=cj, bapT=bapT:
                            e.transpose(bapT[:, cj * 8:(cj + 1) * 8],
                                        s7sb[0:8, cj * P:(cj + 1) * P], iden[0:8, 0:8])),
                            inc="t" if cj == 3 else None,
                            waits=[e1, ld0[0]] + ([frwT] if frwT and cj == 0 else [])))
        e2 = vop(lambda e, bapT=bapT: e.tensor_copy(s7tm[:, :], bapT[:, 0:32]),
                 waits=[ttoks[-1]])
        bk.set_free(biT, e2)
        e3 = vop(lambda e: e.activation(
            sg4[:, :].rearrange("p (c f) -> p c f", f=4),
            s7tm[:, :].rearrange("p (c f) -> p c f", f=8)[:, :, 0:4], AF.Sigmoid),
            eng="scalar", waits=[e2])
        bts = []
        for e_i in range(3):
            b1_ = vop(lambda e, e_i=e_i: e.tensor_tensor(
                bidb[:, :].rearrange("p (c f) -> p c f", f=3)[:, :, e_i:e_i + 1],
                sg4[:, :].rearrange("p (c f) -> p c f", f=4)[:, :, e_i:e_i + 1],
                sg4[:, :].rearrange("p (c f) -> p c f", f=4)[:, :, 3:4], ALU.mult),
                waits=[e3])
            bts.append(b1_)
        for e_i in range(3):
            vop(lambda e, e_i=e_i: e.tensor_scalar(
                bidb[:, :].rearrange("p (c f) -> p c f", f=3)[:, :, e_i:e_i + 1],
                bidb[:, :].rearrange("p (c f) -> p c f", f=3)[:, :, e_i:e_i + 1],
                capsB[:, e_i:e_i + 1], None, ALU.mult), waits=[t_capb])
        for e_i in range(3):
            vop(lambda e, e_i=e_i: e.tensor_tensor(
                bidb[:, :].rearrange("p (c f) -> p c f", f=3)[:, :, e_i:e_i + 1],
                bidb[:, :].rearrange("p (c f) -> p c f", f=3)[:, :, e_i:e_i + 1],
                s7tm[:, :].rearrange("p (c f) -> p c f", f=8)[:, :, 4 + e_i:5 + e_i],
                ALU.add))
        vpad()
        def bid(e_i):
            return bidb[:, :].rearrange("p (c f) -> p c f", f=3)[:, :, e_i:e_i + 1]
        def mk(e_i):
            return mskc[:, :].rearrange("p (c f) -> p c f", f=3)[:, :, e_i:e_i + 1]
        cmp_ops = [(0, 1, ALU.is_ge), (1, 0, ALU.is_gt), (2, 0, ALU.is_gt)]
        for (ia, ib, op_), e_i in zip(cmp_ops, range(3)):
            vop(lambda e, ia=ia, ib=ib, op_=op_, e_i=e_i: e.tensor_tensor(
                mk(e_i), bid(ia), bid(ib), op_))
        cmp2 = [(0, 2, ALU.is_ge), (1, 2, ALU.is_ge), (2, 1, ALU.is_gt)]
        for (ia, ib, op_), e_i in zip(cmp2, range(3)):
            vop(lambda e, ia=ia, ib=ib, op_=op_, e_i=e_i: e.tensor_tensor(
                sg4[:, :].rearrange("p (c f) -> p c f", f=4)[:, :, e_i:e_i + 1],
                bid(ia), bid(ib), op_))
        vpad()
        for e_i in range(3):
            vop(lambda e, e_i=e_i: e.tensor_tensor(
                mk(e_i), mk(e_i),
                sg4[:, :].rearrange("p (c f) -> p c f", f=4)[:, :, e_i:e_i + 1], ALU.mult))
        vpad()
        e3b = vop(lambda e: e.activation(
            sg4[:, :].rearrange("p (c f) -> p c f", f=4),
            s7tm[:, :].rearrange("p (c f) -> p c f", f=8)[:, :, 0:4], AF.Sigmoid),
            eng="scalar")
        wcs = []
        for e_i in range(3):
            wcs.append(vop(lambda e, e_i=e_i: e.tensor_tensor(
                bid(e_i), mk(e_i),
                sg4[:, :].rearrange("p (c f) -> p c f", f=4)[:, :, e_i:e_i + 1], ALU.mult),
                waits=[e3b]))
        vpad()
        vop(lambda e: e.tensor_tensor(bid(0), bid(0), bid(1), ALU.add))
        vpad()
        vop(lambda e: e.tensor_tensor(bid(0), bid(0), bid(2), ALU.add))
        vpad()
        vop(lambda e: e.tensor_scalar(bid(1), bid(0), 1e-6, None, ALU.add))
        vpad()
        vop(lambda e: e.reciprocal(bid(2), bid(1)))
        vpad()
        vop(lambda e: e.tensor_tensor(bid(0), bid(0), bid(2), ALU.mult))
        vpad()
        for e_i in range(3):
            vop(lambda e, e_i=e_i: e.tensor_tensor(mk(e_i), mk(e_i), bid(0), ALU.mult))
        vpad()
        mfin = vop(lambda e: e.nop(cycle_cnt=64))
        biM, bapM, frwM = bk.take()
        mtoks = []
        for cj in range(4):
            mtoks.append(OP("tensor", (lambda e, cj=cj, bapM=bapM:
                            e.transpose(bapM[0:3, cj * P:(cj + 1) * P],
                                        mskc[:, cj * 3:(cj + 1) * 3], iden[:, :])),
                            inc="t" if cj == 3 else None,
                            waits=[mfin] + ([frwM] if frwM and cj == 0 else [])))
        e4 = vop(lambda e, bapM=bapM: e.tensor_copy(s7sb[0:3, :], bapM[0:3, 0:TS]),
                 waits=[mtoks[-1]])
        bk.set_free(biM, e4)
        g1 = None
        for e_i in range(3):
            g1 = dma("sync", lambda e, e_i=e_i: e.dma_start(
                out=mrow[0:1, e_i * TS:(e_i + 1) * TS],
                in_=s7sb[e_i:e_i + 1, :]), waits=[e4])
        for e_i in range(3):
            biB, bapB, frwB = bk.take()
            tb = mmg(biB, bapB, [(onec[0:1, :], mrow[0:1, e_i * TS:(e_i + 1) * TS])],
                     waits=[g1, mem1], free_wait=frwB)
            eb = vop(lambda e, bapB=bapB, ts_i=ts_i, e_i=e_i: e.tensor_copy(
                mskB[:, (ts_i * 3 + e_i) * TS:(ts_i * 3 + e_i + 1) * TS], bapB[:, :]),
                waits=[tb])
            bk.set_free(biB, eb)
            msk_done[ts_i] = eb
    tap("Q_mskB", mskB[:, :], [P, NSUB * 3 * TS], [msk_done[NSUB - 1]], BF16)

    def mskap(ts_i, e_i):
        return mskB[:, (ts_i * 3 + e_i) * TS:(ts_i * 3 + e_i + 1) * TS]

    sa.freet("x2b", "s7sb", "s7tm", "sg4", "bidb", "mskc", "mrow")
    pbuf = sa.alloc("pbuf", [P, G * NT], BF16)
    ld_w1f = dma("gpsimd", lambda e: e.dma_start(**wload_args(wA[:, :], dw["fW1"], D)),
                 waits=[mm_s, w7ld, x2_sp])
    ld_w2f = dma("gpsimd", lambda e: e.dma_start(**wload_args(wB[:, :], dw["fW2"], D)),
                 waits=[mm_o2])
    _, cv_w1f_tok = cvec_mms(wA, D, ppb["fln2b"], cv_W1f, 0, 8, [ld_w1f, pp_done["b_fln2b"]])
    p_toks = {}
    mm_p_last = None
    prev_pv = None
    for ts_i in range(NSUB):
        for oo in range(G):
            bi1, bap1, frw1 = bk.take()
            bi2, bap2, frw2 = bk.take()
            p1_ = [(wA[:, k * D + oo * P:k * D + oo * P + P], slc(hw, k, ts_i)) for k in range(8)]
            p2_ = [(wB[:, k * D + oo * P:k * D + oo * P + P], slc(state, k, ts_i)) for k in range(8)]
            w_ = [ld_w1f, hw_done] + ([frw1] if frw1 else [])
            tok1 = mmg(bi1, bap1, p1_, waits=w_)
            w_ = [ld_w2f] + [st_toks[(k, ts_i)] for k in range(8)] + ([frw2] if frw2 else [])
            tok2 = mmg(bi2, bap2, p2_, waits=w_)
            mm_p_last = tok2
            t2 = vop(lambda e, bap=bap2: e.activation(
                S2["TT"][:, 0:TS], bap[:, :], AF.Sigmoid), eng="scalar",
                waits=[tok2] + ([prev_pv] if prev_pv else []))
            pv = vop(lambda e, bap=bap1, oo=oo, ts_i=ts_i: e.scalar_tensor_tensor(
                slc(pbuf, oo, ts_i), bap[:, :], cv_W1f[:, oo:oo + 1], S2["TT"][:, 0:TS],
                ALU.add, ALU.mult), waits=[tok1, cv_w1f_tok, t2])
            p_toks[(oo, ts_i)] = pv
            prev_pv = pv
            bk.set_free(bi1, pv)
            bk.set_free(bi2, t2)
    sa.freet("state")
    finalacc = sa.alloc("finalacc", [P, G * NT], BF16)
    facc_toks = {}
    ld_w3f = dma("gpsimd", lambda e: e.dma_start(**wload_args(wA[:, :], dw["fW3"], D)),
                 waits=[mm_p_last])
    mm_w3f = None
    for ts_i in range(NSUB):
        for oo in range(G):
            bi, bap, frw = bk.take()
            pairs = [(wA[:, k * D + oo * P:k * D + oo * P + P],
                      slc(pbuf, k, ts_i)) for k in range(8)]
            w_ = [ld_w3f] + [p_toks[(k, ts_i)] for k in range(8)] + ([frw] if frw else [])
            tok = mmg(bi, bap, pairs, waits=w_)
            mm_w3f = tok
            t = vop(lambda e, bap=bap, oo=oo, ts_i=ts_i: e.tensor_tensor(
                slc(finalacc, oo, ts_i), bap[:, :], mskap(ts_i, 2), ALU.mult),
                waits=[tok, msk_done[ts_i]])
            facc_toks[(oo, ts_i)] = t
            bk.set_free(bi, t)

    sa.freet("pbuf", "wA", "wB")
    wK = sa.alloc("wK", [P, 8 * FH], BF16)
    wV = sa.alloc("wV", [P, 16 * D], BF16)
    hr2 = sa.alloc("hr2", [P, 16 * TS], BF16)
    prev_k_mm = mm_w3f
    prev_v_mm = mm_w3f
    for e_i in range(2):
        for half in range(2):
            kld = dma("gpsimd", lambda e, e_i=e_i, half=half: e.dma_start(
                **wload_args(wK[:, :], dK[e_i, :, half * FH:(half + 1) * FH], FH)),
                waits=[prev_k_mm])
            vld = dma("gpsimd", lambda e, e_i=e_i, half=half: e.dma_start(
                **wload_args(wV[:, :], dV[e_i, half * FH:(half + 1) * FH, :], D)),
                waits=[prev_v_mm])
            _, cvK = cvec_mms(wK, FH, ppb["fln2b"], cv_K[e_i], half * 16, 16, [kld, pp_done["b_fln2b"]])
            last_vmm = None
            prev_sq = None
            for ts_i in range(NSUB):
                prev_vmm_ts = last_vmm
                relu_toks = {}
                for og_h in range(16):
                    bi, bap, frw = bk.take()
                    pairs = [(wK[:, k * FH + og_h * P:k * FH + og_h * P + P],
                              slc(hw, k, ts_i)) for k in range(8)]
                    w_ = [kld, hw_done]
                    if frw: w_.append(frw)
                    tok = mmg(bi, bap, pairs, waits=w_)
                    rw = [tok, cvK]
                    if prev_vmm_ts is not None:
                        rw.append(prev_vmm_ts)
                    if prev_sq is not None:
                        rw.append(prev_sq)
                    rt = vop(lambda e, bap=bap, og_h=og_h, e_i=e_i, half=half: e.activation(
                        S2["TT"][:, 0:TS], bap[:, :], AF.Relu,
                        bias=cv_K[e_i][:, half * 16 + og_h:half * 16 + og_h + 1]),
                        eng="scalar", waits=rw)
                    sq = vop(lambda e, og_h=og_h: e.tensor_tensor(
                        hr2[:, og_h * TS:(og_h + 1) * TS], S2["TT"][:, 0:TS],
                        S2["TT"][:, 0:TS], ALU.mult), waits=[rt])
                    relu_toks[og_h] = sq
                    prev_sq = sq
                    bk.set_free(bi, sq)
                for oo in range(G):
                    bi, bap, frw = bk.take()
                    pairs = [(wV[:, kh * D + oo * P:kh * D + oo * P + P],
                              hr2[:, kh * TS:(kh + 1) * TS]) for kh in range(16)]
                    w_ = [vld] + [relu_toks[kh] for kh in range(16)]
                    if frw: w_.append(frw)
                    tok = mmg(bi, bap, pairs, waits=w_)
                    last_vmm = tok
                    t1_ = vop(lambda e, bap=bap, ts_i=ts_i, e_i=e_i: e.tensor_tensor(
                        S2["TT2"][:, 0:TS], bap[:, :], mskap(ts_i, e_i), ALU.mult),
                        waits=[tok, msk_done[ts_i]])
                    t2_ = vop(lambda e, oo=oo, ts_i=ts_i: e.tensor_tensor(
                        slc(finalacc, oo, ts_i), slc(finalacc, oo, ts_i),
                        S2["TT2"][:, 0:TS], ALU.add), waits=[facc_toks[(oo, ts_i)]])
                    facc_toks[(oo, ts_i)] = t2_
                    bk.set_free(bi, t1_)
            prev_k_mm = last_vmm
            prev_v_mm = last_vmm
    facc_all = facc_toks[(G - 1, NSUB - 1)]
    tap("Q_facc", finalacc[:, :], [P, G * NT], [facc_all], BF16)

    sa.freet("hw", "wK", "wV", "hr2", "w7sb")
    fo = sa.alloc("fo", [P, G * NT], BF16)
    x2r = sa.alloc("x2r", [P, G * NT], BF16)
    x2_rl = dma("scalar", lambda e: e.dma_start(
        out=x2r[:, :].rearrange("p (g t) -> p g t", t=NT),
        in_=d_x2.rearrange("g p t -> p g t")), waits=[x2_sp, facc_all])
    fo_done = {}
    for ts_i in range(NSUB):
        for og in range(G):
            fo_done[(og, ts_i)] = vop(lambda e, og=og, ts_i=ts_i: e.tensor_tensor(
                slc(fo, og, ts_i), slc(x2r, og, ts_i), slc(finalacc, og, ts_i), ALU.add),
                waits=[facc_toks[(og, ts_i)], x2_rl])
    o3_rows = fm_ln_stats(lambda og, ts_i: slc(fo, og, ts_i),
                          lambda og, ts_i: [fo_done[(og, ts_i)]], 1e-5)
    x1r = sa.alloc("x1r", [P, G * NT], BF16)
    x1_rl = dma("scalar", lambda e: e.dma_start(
        out=x1r[:, :].rearrange("p (g t) -> p g t", t=NT),
        in_=d_x1.rearrange("g p t -> p g t")), waits=[x1_sp])
    xoutb = sa.alloc("xoutb", [P, G * TS], F32)
    outtm = sa.alloc("outtm", [P, 2 * D], F32)
    out_dmas = []
    prev_tt_last = None
    for ts_i in range(NSUB):
        bi1, bap1, frw1 = bk.take()
        bi2, bap2, frw2 = bk.take()
        t1 = mmg(bi1, bap1, [(onecb[0:1, :], rbf_r[0:1, ts_i * TS:(ts_i + 1) * TS])],
                 waits=list(o3_rows) + [mem2], free_wait=frw1)
        t2 = mmg(bi2, bap2, [(onecb[0:1, :], rbf_m[0:1, ts_i * TS:(ts_i + 1) * TS])],
                 waits=list(o3_rows) + [mem2], free_wait=frw2)
        ln_bcast_last[0] = t2
        xo_toks = []
        for og in range(G):
            v1 = vop(lambda e, og=og, ts_i=ts_i, bap=bap1: e.scalar_tensor_tensor(
                S2["TT"][:, 0:TS], slc(fo, og, ts_i), 1.0, bap[:, :], ALU.bypass, ALU.mult),
                waits=[t1, t2])
            v2 = vop(lambda e, og=og, ts_i=ts_i, bap=bap2: e.scalar_tensor_tensor(
                S2["TT2"][:, 0:TS], bap[:, :], 1.0, S2["TT"][:, 0:TS], ALU.bypass, ALU.add))
            v3 = vop(lambda e, og=og, ts_i=ts_i: e.tensor_tensor(
                xoutb[:, og * TS:(og + 1) * TS], S2["TT2"][:, 0:TS],
                slc(x1r, og, ts_i), ALU.add),
                waits=[x1_rl] + ([prev_tt_last] if prev_tt_last else []))
            xo_toks.append(v3)
        bk.set_free(bi1, xo_toks[-1])
        bk.set_free(bi2, xo_toks[-1])
        for cj in range(4):
            ch = ts_i * 4 + cj
            bo = ch % 2
            biA, bapA, frwA = bk.take()
            biB2, bapB2, frwB2 = bk.take()
            ttoks = []
            for og in range(G):
                bb = bapA if og < 4 else bapB2
                fw = []
                if og == 0 and frwA: fw.append(frwA)
                if og == 4 and frwB2: fw.append(frwB2)
                ttoks.append(OP("tensor", (lambda e, og=og, cj=cj, bb=bb:
                                e.transpose(bb[:, (og % 4) * P:(og % 4 + 1) * P],
                                            xoutb[:, og * TS + cj * P: og * TS + cj * P + P],
                                            iden[:, :])),
                                inc="t" if og in (3, 7) else None,
                                waits=[xo_toks[og], ld0[0]] + fw))
            prev_tt_last = ttoks[7]
            wprev = [out_dmas[-2]] if len(out_dmas) >= 2 else []
            evA = vop(lambda e, bapA=bapA, bo=bo: e.tensor_copy(
                outtm[:, bo * D:bo * D + TS], bapA[:, :]), waits=[ttoks[3]] + wprev)
            evB = vop(lambda e, bapB2=bapB2, bo=bo: e.tensor_copy(
                outtm[:, bo * D + TS:bo * D + 2 * TS], bapB2[:, :]), waits=[ttoks[7]] + wprev)
            bk.set_free(biA, evA)
            bk.set_free(biB2, evB)
            w_dma = [evA, evB]
            if len(out_dmas) >= 2:
                pass
            out_dmas.append(dma("scalar", lambda e, ch=ch, bo=bo: e.dma_start(
                out=dout[ch * P:(ch + 1) * P, :], in_=outtm[:, bo * D:(bo + 1) * D]),
                waits=w_dma))
    OP("sync", lambda e: e.nop(), waits=[out_dmas[-1]])

    sems = {}
    import contextlib
    stack = contextlib.ExitStack()
    for s in ("t", "v", "s", "g", "dg", "ds", "dy", "cc"):
        sems[s] = stack.enter_context(nc.semaphore(s))
    with nc.Block() as block:
        block.tensor(lambda e: em.replay("tensor", e, sems))
        block.vector(lambda e: em.replay("vector", e, sems))
        block.scalar(lambda e: em.replay("scalar", e, sems))
        block.gpsimd(lambda e: em.replay("gpsimd", e, sems))
        block.sync(lambda e: em.replay("sync", e, sems))
    stack.close()
    return nc, tap_tensors


def kernel(**inputs):
    x = np.asarray(inputs["L0_attn_x"])
    ins = {}
    for nm, key in [("Wq", "L0_attn_Wq"), ("Wk", "L0_attn_Wk"), ("Wv", "L0_attn_Wv"),
                    ("Wg", "L0_attn_Wg"), ("Wa", "L0_attn_Wa"), ("Wo", "L0_attn_Wo"),
                    ("W1", "L0_attn_W1"), ("W2", "L0_attn_W2"),
                    ("fWr", "L0_ffn_Wr"), ("fWv", "L0_ffn_Wv"), ("fWo", "L0_ffn_Wo"),
                    ("fWs", "L0_ffn_Ws"), ("fW1", "L0_ffn_W1"), ("fW2", "L0_ffn_W2"),
                    ("fW3", "L0_ffn_W3"), ("fK", "L0_ffn_K_rwkv"), ("fV", "L0_ffn_V_rwkv"),
                    ("b1", "L0_attn_b1"), ("b2", "L0_attn_b2"),
                    ("ln1s", "L0_attn_ln1_scale"), ("ln1b", "L0_attn_ln1_bias"),
                    ("ln2s", "L0_attn_ln2_scale"), ("ln2b", "L0_attn_ln2_bias"),
                    ("fln1w", "L0_ffn_ln1_w"), ("fln1b", "L0_ffn_ln1_b"),
                    ("fln2w", "L0_ffn_ln2_w"), ("fln2b", "L0_ffn_ln2_b"),
                    ("caps", "L0_ffn_capital_shares")]:
        ins[nm] = np.ascontiguousarray(np.asarray(inputs[key]), dtype=np.float32)
    cr = np.asarray(inputs["L0_ffn_conf_rwkv"])
    ins["W7h"] = np.ascontiguousarray(np.stack(
        [cr[0], cr[1], np.asarray(inputs["L0_ffn_conf_trans"]),
         np.asarray(inputs["L0_ffn_w_diff"]),
         np.asarray(inputs["L0_ffn_W_aff"])[:, 0],
         np.asarray(inputs["L0_ffn_W_aff"])[:, 1],
         np.asarray(inputs["L0_ffn_W_aff"])[:, 2],
         np.zeros(D, np.float32)], axis=1), dtype=np.float32)
    ins["iden"] = np.eye(P, dtype=np.float32)

    nc, _ = build_nc()
    in_maps = []
    for c in range(8):
        b, qq = c // 4, c % 4
        m = dict(ins)
        m["x"] = np.ascontiguousarray(x[b, qq * NT:(qq + 1) * NT, :], dtype=np.float32)
        sel = np.zeros((P, 8), np.float32)
        for s_i in range(8):
            if s_i // 4 == b and s_i < c:
                sel[:, s_i] = 1.0
        m["selv"] = sel
        in_maps.append(m)
    res = run_bass_kernel_spmd(nc, in_maps, core_ids=list(range(8)))
    out = np.empty((2, 8192, D), np.float32)
    for c in range(8):
        b, qq = c // 4, c % 4
        out[b, qq * NT:(qq + 1) * NT, :] = res.results[c]["out"]
    return out
